# version 02
# baseline (speedup 1.0000x reference)
import numpy as np
import concourse.bass as bass
import concourse.mybir as mybir
from concourse.bass_utils import run_bass_kernel_spmd

F32 = mybir.dt.float32
BF16 = mybir.dt.bfloat16
AF = mybir.ActivationFunctionType
ALU = mybir.AluOpType
AX = mybir.AxisListType

H = 2048
HC = 16
NE = 16
NI = 11
NSI = 22
HEADS = 16
DH = 128
EPS = 1e-5
SHIFT = 4.0
SCALE = float(1.0 / np.sqrt(DH))

NCORES = 8
TPC = 2048
HALF = TPC // 2


def _set_dims(tpc, ncores):
    global TPC, HALF, NCORES
    TPC, HALF, NCORES = tpc, tpc // 2, ncores


def _tslices(n):
    return [(a, min(a + 512, n)) for a in range(0, n, 512)]


def _stage_all(inp):
    st = {}
    up = np.asarray(inp["L0_ffn_expert_up_w"], np.float32)
    st["up"] = np.ascontiguousarray(
        up.reshape(NE, 2, NI, 128, HC, 128).transpose(0, 2, 5, 1, 4, 3))
    dn = np.asarray(inp["L0_ffn_expert_down_w"], np.float32)
    st["dn"] = np.ascontiguousarray(
        dn.reshape(NE, HC, 128, NI, 128).transpose(0, 1, 4, 3, 2))
    sup = np.asarray(inp["L0_ffn_shared_up_w"], np.float32)
    st["sup"] = np.ascontiguousarray(
        sup.reshape(2, NSI, 128, HC, 128).transpose(1, 4, 0, 3, 2))
    sdn = np.asarray(inp["L0_ffn_shared_down_w"], np.float32)
    st["sdn"] = np.ascontiguousarray(
        sdn.reshape(HC, 128, NSI, 128).transpose(0, 3, 2, 1))
    st["gate"] = np.ascontiguousarray(
        np.asarray(inp["L0_ffn_gate_w"], np.float32).T.reshape(HC, 128, NE)
        .transpose(1, 0, 2).reshape(128, HC * NE))
    for nm in ("wq", "wk", "wo"):
        w = np.asarray(inp[f"L0_attn_{nm}"], np.float32)
        st[nm] = np.ascontiguousarray(
            w.reshape(HC, 128, HC, 128).transpose(0, 3, 2, 1))
    wv = np.asarray(inp["L0_attn_wv"], np.float32)
    st["wv"] = np.ascontiguousarray(
        wv.reshape(4, 512, HC, 128).transpose(0, 3, 2, 1))
    for nm in ("bq", "bk", "bo"):
        st[nm] = np.ascontiguousarray(
            np.asarray(inp[f"L0_attn_{nm}"], np.float32).reshape(HC, 128, 1))
    st["bv"] = np.ascontiguousarray(
        np.asarray(inp["L0_attn_bv"], np.float32).reshape(1, H))
    st["ident"] = np.eye(128, dtype=np.float32)
    st["ones_row"] = np.ones((1, 128), np.float32)
    st["ones_col"] = np.ones((128, 1), np.float32)
    return st


ENGS = ("sync", "gpsimd", "tensor", "vector", "scalar")


class Sched:
    def __init__(self, nc):
        self.nc = nc
        self.steps = {e: [] for e in ENGS}
        self.sems = {}
        self.cnt = {}

    def sem(self, name):
        if name not in self.sems:
            self.sems[name] = self.nc.alloc_semaphore(name)
            self.cnt[name] = 0
        return name

    def ev(self, name):
        self.sem(name)
        return (name, self.cnt[name])

    def step(self, eng, fn, waits=(), inc=None, amt=1):
        waits = tuple(w for w in waits if w is not None)
        if inc is not None:
            self.sem(inc)
            self.cnt[inc] += amt
        self.steps[eng].append((fn, waits, inc, amt))
        return self.ev(inc) if inc is not None else None

    def emit(self, block):
        sems, steps = self.sems, self.steps

        def runner(eng_name):
            def run(eng):
                for fn, waits, inc, amt in steps[eng_name]:
                    for (s, v) in waits:
                        if v > 0:
                            eng.wait_ge(sems[s], v)
                    ins = fn(eng)
                    if inc is not None:
                        assert ins is not None, f"step with inc returned None on {eng_name}"
                        ins.then_inc(sems[inc], amt)
            return run

        block.sync(runner("sync"))
        block.gpsimd(runner("gpsimd"))
        block.tensor(runner("tensor"))
        block.vector(runner("vector"))
        block.scalar(runner("scalar"))


def alloc_common(nc, nb_min=0):
    o = type("NS", (), {})()
    A = nc.alloc_sbuf_tensor
    o.AF = A("arena_f32", [128, HC * HALF], F32)
    nb = HC * HALF + NI * HALF + 2 * (2 * HC * 128) + 2 * (NI * 128)
    nb = max(nb + 2 * HC * 128 + HC * 256 + 3 * HALF, nb_min)
    o.AB = A("arena_bf16", [128, nb], BF16)

    def bview(a, b):
        return o.AB[:, a:b]
    o._bv = bview
    c0 = 0
    o.x_sb = o.AF[:, 0:HC * HALF]
    o.xn_sb = bview(c0, c0 + HC * HALF); c0 += HC * HALF
    o.a_sb = bview(c0, c0 + NI * HALF); c0 += NI * HALF
    o.ug_sb = []
    for j in range(2):
        o.ug_sb.append(bview(c0, c0 + 2 * HC * 128)); c0 += 2 * HC * 128
    o.dn_sb = []
    for j in range(2):
        o.dn_sb.append(bview(c0, c0 + NI * 128)); c0 += NI * 128
    o.bf_used = c0
    o.gate_sb = A("gate_sb", [128, HC * NE], BF16)
    o.sg_sb = A("sg_sb", [128, HALF], F32)
    o.t2_sb = A("t2_sb", [128, HALF], F32)
    o.cwB_sb = A("cwB_sb", [128, HALF], F32)
    o.cwT_sb = A("cwT_sb", [NE, HALF], F32)
    o.xb_sb = [A(f"xb{j}", [128, HALF], BF16) for j in range(2)]
    o.xq_sb = [A(f"xq{j}", [128, HALF], BF16) for j in range(2)]
    o.onr_sb = A("onr_sb", [1, 128], F32)
    o.onc_sb = A("onc_sb", [128, 1], BF16)
    o.id_sb = A("id_sb", [128, 128], F32)
    o.m_row = o.t2_sb[0:1, 0:HALF]
    o.v_row = o.sg_sb[0:1, 0:HALF]
    o.r_row = o.cwB_sb[0:1, 0:HALF]
    o.t_row = A("t_row", [1, HALF], F32)
    o.mrow = A("g_mrow", [128, 1], F32)
    o.strow = A("g_st", [128, NE], F32)
    o.gmax = A("g_gmax", [128, 4], F32)
    o.ggt = A("g_ggt", [128, 16], F32)
    o.grank = A("g_grank", [128, 4], F32)
    o.gmask = A("g_gmask", [128, 4], F32)
    o.msk = A("g_msk", [128, NE], F32)
    o.ert = A("g_ert", [128, NE * NE], F32)
    o.rank = A("g_rank", [128, NE], F32)
    o.sel = A("g_sel", [128, NE], F32)
    o.cwr = A("g_cwr", [128, NE], F32)
    o.wsum = A("g_wsum", [128, 1], F32)
    o.cw = A("g_cw", [128, NE], F32)
    o.gcol = [A(f"g_gcol{j}", [128, 1], F32) for j in range(4)]
    o.mcol = [A(f"g_mcol{j}", [128, 1], F32) for j in range(NE)]
    P = nc.alloc_psum_tensor
    o.P1 = P("P1", [128, min(HALF, 1024)], F32)
    o.P2 = P("P2", [128, min(HALF, 1024)], F32)
    o.P3 = P("P3", [128, min(HALF, 1024)], F32)
    o.P4 = P("P4", [128, 128], F32)
    o.P5 = P("P5", [NE, 128], F32)
    return o


def emit_ln(b, o, x_ev):
    V, S, T = "vector", "scalar", "tensor"
    sv, ss, st = b.sem("sv"), b.sem("ss"), b.sem("st")

    pe_done = {0: b.ev(st), 1: b.ev(st)}
    for hc in range(HC):
        j = hc % 2
        xsl = o.x_sb[:, hc * HALF:(hc + 1) * HALF]
        cast_ev = b.step(V, lambda e, j=j, xsl=xsl: e.tensor_copy(o.xb_sb[j][:, :], xsl),
                         waits=(pe_done[j], x_ev), inc=sv)
        sq_ev = b.step(S, lambda e, j=j, xsl=xsl: e.square(out=o.xq_sb[j][:, :], in_=xsl),
                       waits=(pe_done[j], x_ev), inc=ss)

        def mm_stats(e, j=j, hc=hc):
            ins = None
            for (a0, a1) in _tslices(HALF):
                ins = e.matmul(o.P1[0:1, a0:a1], o.onc_sb[:, :], o.xb_sb[j][:, a0:a1],
                               start=(hc == 0), stop=(hc == HC - 1))
            for (a0, a1) in _tslices(HALF):
                ins = e.matmul(o.P2[0:1, a0:a1], o.onc_sb[:, :], o.xq_sb[j][:, a0:a1],
                               start=(hc == 0), stop=(hc == HC - 1))
            return ins
        pe_done[j] = b.step(T, mm_stats, waits=(cast_ev, sq_ev), inc=st)

    stats_ev = pe_done[(HC - 1) % 2]
    e1 = b.step(V, lambda e: e.tensor_single_scalar(out=o.m_row[:, :], in_=o.P1[0:1, :],
                                                    scalar=1.0 / H, op=ALU.mult),
                waits=(stats_ev,), inc="sv")
    e2 = b.step(V, lambda e: e.tensor_single_scalar(out=o.v_row[:, :], in_=o.P2[0:1, :],
                                                    scalar=1.0 / H, op=ALU.mult),
                waits=(stats_ev,), inc="sv")
    e3 = b.step(V, lambda e: e.tensor_tensor(out=o.t_row[:, :], in0=o.m_row[:, :],
                                             in1=o.m_row[:, :], op=ALU.mult),
                waits=(e1,), inc="sv")
    e4 = b.step(V, lambda e: e.tensor_tensor(out=o.v_row[:, :], in0=o.v_row[:, :],
                                             in1=o.t_row[:, :], op=ALU.subtract),
                waits=(e2, e3), inc="sv")
    e5 = b.step(V, lambda e: e.tensor_single_scalar(out=o.v_row[:, :], in_=o.v_row[:, :],
                                                    scalar=EPS, op=ALU.add),
                waits=(e4,), inc="sv")
    e6 = b.step(S, lambda e: e.activation(out=o.t_row[:, :], in_=o.v_row[:, :], func=AF.Sqrt),
                waits=(e5,), inc="ss")
    e7 = b.step(V, lambda e: e.reciprocal(out=o.r_row[:, :], in_=o.t_row[:, :]),
                waits=(e6,), inc="sv")
    e8 = b.step(V, lambda e: e.tensor_tensor(out=o.t_row[:, :], in0=o.r_row[:, :],
                                             in1=o.r_row[:, :], op=ALU.mult),
                waits=(e7,), inc="sv")
    e9 = b.step(V, lambda e: e.tensor_tensor(out=o.t_row[:, :], in0=o.t_row[:, :],
                                             in1=o.v_row[:, :], op=ALU.mult),
                waits=(e8,), inc="sv")
    e10 = b.step(V, lambda e: e.tensor_single_scalar(out=o.t_row[:, :], in_=o.t_row[:, :],
                                                     scalar=-0.5, op=ALU.mult),
                 waits=(e9,), inc="sv")
    e10b = b.step(V, lambda e: e.tensor_single_scalar(out=o.t_row[:, :], in_=o.t_row[:, :],
                                                      scalar=1.5, op=ALU.add),
                  waits=(e10,), inc="sv")
    e11 = b.step(V, lambda e: e.tensor_tensor(out=o.r_row[:, :], in0=o.r_row[:, :],
                                              in1=o.t_row[:, :], op=ALU.mult),
                 waits=(e10b,), inc="sv")

    def mm_bcast(e):
        ins = None
        for (a0, a1) in _tslices(HALF):
            ins = e.matmul(o.P1[:, a0:a1], o.onr_sb[:, :], o.m_row[0:1, a0:a1],
                           start=True, stop=True)
        for (a0, a1) in _tslices(HALF):
            ins = e.matmul(o.P2[:, a0:a1], o.onr_sb[:, :], o.r_row[0:1, a0:a1],
                           start=True, stop=True)
        return ins
    bc_ev = b.step(T, mm_bcast, waits=(e11,), inc="st")

    sub_t = [o.sg_sb, o.t2_sb]
    last_mul = None
    for hc0 in range(0, HC, 2):
        waits = (bc_ev, last_mul)
        b.step(V, lambda e, hc=hc0: e.tensor_tensor(
            out=sub_t[0][:, :], in0=o.x_sb[:, hc * HALF:(hc + 1) * HALF],
            in1=o.P1[:, :], op=ALU.subtract), waits=waits)
        sub_ev = b.step(V, lambda e, hc=hc0 + 1: e.tensor_tensor(
            out=sub_t[1][:, :], in0=o.x_sb[:, hc * HALF:(hc + 1) * HALF],
            in1=o.P1[:, :], op=ALU.subtract), waits=waits, inc="sv")
        b.step(V, lambda e, hc=hc0: e.tensor_tensor(
            out=o.xn_sb[:, hc * HALF:(hc + 1) * HALF], in0=sub_t[0][:, :],
            in1=o.P2[:, :], op=ALU.mult), waits=(sub_ev,))
        last_mul = b.step(V, lambda e, hc=hc0 + 1: e.tensor_tensor(
            out=o.xn_sb[:, hc * HALF:(hc + 1) * HALF], in0=sub_t[1][:, :],
            in1=o.P2[:, :], op=ALU.mult), waits=(sub_ev,), inc="sv")
    return last_mul


def emit_gating(b, o, xn_ev):
    V, S, T = "vector", "scalar", "tensor"
    p4_free = b.ev("st")
    p5_free = b.ev("sv")
    for tc in range(HALF // 128):
        def mm_logits(e, tc=tc):
            ins = None
            for kf in range(HC):
                ins = e.matmul(o.P4[:, 0:NE],
                               o.xn_sb[:, kf * HALF + tc * 128: kf * HALF + (tc + 1) * 128],
                               o.gate_sb[:, kf * NE:(kf + 1) * NE],
                               start=(kf == 0), stop=(kf == HC - 1))
            return ins
        lg_ev = b.step(T, mm_logits, waits=(xn_ev, p4_free), inc="st")
        b.step(V, lambda e: e.reduce_max(out=o.mrow[:, :], in_=o.P4[:, 0:NE], axis=AX.X),
               waits=(lg_ev,))
        m2 = b.step(V, lambda e: e.tensor_single_scalar(out=o.mrow[:, :], in_=o.mrow[:, :],
                                                        scalar=-1.0, op=ALU.mult), inc="sv")
        st_ev = b.step(S, lambda e: e.activation(out=o.strow[:, :], in_=o.P4[:, 0:NE],
                                                 func=AF.Exp, bias=o.mrow[:, :], scale=1.0),
                       waits=(m2,), inc="ss")
        p4_free = st_ev
        g1 = b.step(V, lambda e: e.tensor_reduce(
            out=o.gmax[:, :], in_=o.strow[:, :].rearrange("p (g e) -> p g e", g=4),
            axis=AX.X, op=ALU.max), waits=(st_ev,), inc="sv")
        for g_ in range(4):
            last = b.step(V, lambda e, g_=g_: e.tensor_copy(o.gcol[g_][:, :],
                                                            o.gmax[:, g_:g_ + 1]),
                          waits=(g1,), inc=("sv" if g_ == 3 else None))
        for g_ in range(4):
            last = b.step(V, lambda e, g_=g_: e.tensor_scalar(
                out=o.ggt[:, g_ * 4:(g_ + 1) * 4], in0=o.gmax[:, :],
                scalar1=o.gcol[g_][:, :], scalar2=None, op0=ALU.is_gt),
                waits=(last,), inc=("sv" if g_ == 3 else None))
        gr = b.step(V, lambda e: e.tensor_reduce(
            out=o.grank[:, :], in_=o.ggt[:, :].rearrange("p (g h) -> p g h", g=4),
            axis=AX.X, op=ALU.add), waits=(last,), inc="sv")
        gm = b.step(V, lambda e: e.tensor_single_scalar(out=o.gmask[:, :], in_=o.grank[:, :],
                                                        scalar=2.0, op=ALU.is_lt),
                    waits=(gr,), inc="sv")
        for g_ in range(4):
            last = b.step(V, lambda e, g_=g_: e.tensor_copy(o.gcol[g_][:, :],
                                                            o.gmask[:, g_:g_ + 1]),
                          waits=(gm,), inc=("sv" if g_ == 3 else None))
        for g_ in range(4):
            last = b.step(V, lambda e, g_=g_: e.tensor_scalar(
                out=o.msk[:, g_ * 4:(g_ + 1) * 4], in0=o.strow[:, g_ * 4:(g_ + 1) * 4],
                scalar1=o.gcol[g_][:, :], scalar2=None, op0=ALU.mult),
                waits=(last,), inc=("sv" if g_ == 3 else None))
        mk_ev = last
        for i in range(NE):
            last = b.step(V, lambda e, i=i: e.tensor_copy(o.mcol[i][:, :], o.msk[:, i:i + 1]),
                          waits=(mk_ev,), inc=("sv" if i == NE - 1 else None))
        for i in range(NE):
            last = b.step(V, lambda e, i=i: e.tensor_scalar(
                out=o.ert[:, i * NE:(i + 1) * NE], in0=o.msk[:, :],
                scalar1=o.mcol[i][:, :], scalar2=None, op0=ALU.is_gt),
                waits=(last,), inc=("sv" if i == NE - 1 else None))
        rk = b.step(V, lambda e: e.tensor_reduce(
            out=o.rank[:, :], in_=o.ert[:, :].rearrange("p (i j) -> p i j", i=NE),
            axis=AX.X, op=ALU.add), waits=(last,), inc="sv")
        sl = b.step(V, lambda e: e.tensor_single_scalar(out=o.sel[:, :], in_=o.rank[:, :],
                                                        scalar=4.0, op=ALU.is_lt),
                    waits=(rk,), inc="sv")
        cwr = b.step(V, lambda e: e.tensor_tensor(out=o.cwr[:, :], in0=o.msk[:, :],
                                                  in1=o.sel[:, :], op=ALU.mult),
                     waits=(sl,), inc="sv")
        ws = b.step(V, lambda e: e.reduce_sum(out=o.wsum[:, :], in_=o.cwr[:, :], axis=AX.X),
                    waits=(cwr,), inc="sv")
        ws2 = b.step(V, lambda e: e.tensor_single_scalar(out=o.wsum[:, :], in_=o.wsum[:, :],
                                                         scalar=1e-20, op=ALU.add),
                     waits=(ws,), inc="sv")
        rc = b.step(V, lambda e: e.reciprocal(out=o.wsum[:, :], in_=o.wsum[:, :]),
                    waits=(ws2,), inc="sv")
        cw_ev = b.step(V, lambda e: e.tensor_scalar(out=o.cw[:, :], in0=o.cwr[:, :],
                                                    scalar1=o.wsum[:, :], scalar2=0.5,
                                                    op0=ALU.mult, op1=ALU.mult),
                       waits=(rc,), inc="sv")
        tr_ev = b.step(T, lambda e: e.transpose(o.P5[:, :], o.cw[:, :], o.id_sb[:, :]),
                       waits=(cw_ev, p5_free), inc="st")
        p5_free = b.step(V, lambda e, tc=tc: e.tensor_copy(
            o.cwT_sb[:, tc * 128:(tc + 1) * 128], o.P5[:, :]), waits=(tr_ev,), inc="sv")
    return p5_free


def emit_ffn(b, o, xn_ev, cw_ev, up_ap, dn_ap, sup_ap, sdn_ap):
    V, S, T, G = "vector", "scalar", "tensor", "gpsimd"

    state = dict(
        ug_free=[b.ev("st"), b.ev("st")],
        dn_free=[b.ev("st"), b.ev("st")],
        p1_free=b.ev("ss"), p2_free=b.ev("sv"), p3_free=b.ev("sv"),
        sg_free=b.ev("sv"), t2_free=b.ev("sv"),
    )

    def expert_pass(cwB_scale, n_i, up_ap_i, dn_ap_h, cwb_ev):
        s = state
        a_last = None
        for i in range(n_i):
            j = i % 2
            up_ev = b.step(G, lambda e, i=i, j=j: e.dma_start(
                out=o.ug_sb[j][:, :],
                in_=up_ap_i(i).rearrange("p a k r -> p (a k r)")),
                waits=(s["ug_free"][j],), inc="sw", amt=16)

            def mm_g(e, j=j):
                ins = None
                for kf in range(HC):
                    for (a0, a1) in _tslices(HALF):
                        ins = e.matmul(o.P1[:, a0:a1],
                                       o.ug_sb[j][:, kf * 128:(kf + 1) * 128],
                                       o.xn_sb[:, kf * HALF + a0: kf * HALF + a1],
                                       start=(kf == 0), stop=(kf == HC - 1))
                return ins

            def mm_u(e, j=j):
                ins = None
                for kf in range(HC):
                    for (a0, a1) in _tslices(HALF):
                        ins = e.matmul(o.P2[:, a0:a1],
                                       o.ug_sb[j][:, (HC + kf) * 128:(HC + kf + 1) * 128],
                                       o.xn_sb[:, kf * HALF + a0: kf * HALF + a1],
                                       start=(kf == 0), stop=(kf == HC - 1))
                return ins
            g_ev = b.step(T, mm_g, waits=(up_ev, xn_ev, s["p1_free"]), inc="st")
            u_ev = b.step(T, mm_u, waits=(s["p2_free"],), inc="st")
            s["ug_free"][j] = u_ev
            s["p1_free"] = b.step(S, lambda e: e.activation(
                out=o.sg_sb[:, :], in_=o.P1[:, :], func=AF.Silu),
                waits=(g_ev, s["sg_free"]), inc="ss")
            t2_ev = b.step(V, lambda e: e.tensor_tensor(
                out=o.t2_sb[:, :], in0=o.sg_sb[:, :], in1=o.P2[:, :], op=ALU.mult),
                waits=(s["p1_free"], u_ev, s["t2_free"]), inc="sv")
            s["p2_free"] = t2_ev
            s["sg_free"] = t2_ev
            if cwB_scale is None:
                a_last = b.step(V, lambda e, i=i: e.tensor_tensor(
                    out=o.a_sb[:, i * HALF:(i + 1) * HALF], in0=o.t2_sb[:, :],
                    in1=o.cwB_sb[:, :], op=ALU.mult), waits=(t2_ev, cwb_ev), inc="sv")
            else:
                a_last = b.step(V, lambda e, i=i, sc=cwB_scale: e.tensor_single_scalar(
                    out=o.a_sb[:, i * HALF:(i + 1) * HALF], in_=o.t2_sb[:, :],
                    scalar=sc, op=ALU.mult), waits=(t2_ev,), inc="sv")
            s["t2_free"] = a_last
        a_ev = a_last

        for h in range(HC):
            j = h % 2
            dn_ev = b.step(G, lambda e, h=h, j=j: e.dma_start(
                out=o.dn_sb[j][:, 0:n_i * 128],
                in_=dn_ap_h(h).rearrange("p a r -> p (a r)")),
                waits=(s["dn_free"][j],), inc="sw", amt=16)

            def mm_d(e, j=j, n_i=n_i):
                ins = None
                for ki in range(n_i):
                    for (a0, a1) in _tslices(HALF):
                        ins = e.matmul(o.P3[:, a0:a1],
                                       o.dn_sb[j][:, ki * 128:(ki + 1) * 128],
                                       o.a_sb[:, ki * HALF + a0: ki * HALF + a1],
                                       start=(ki == 0), stop=(ki == n_i - 1))
                return ins
            d_ev = b.step(T, mm_d, waits=(dn_ev, a_ev, s["p3_free"]), inc="st")
            s["dn_free"][j] = d_ev
            s["p3_free"] = b.step(V, lambda e, h=h: e.tensor_tensor(
                out=o.x_sb[:, h * HALF:(h + 1) * HALF],
                in0=o.x_sb[:, h * HALF:(h + 1) * HALF],
                in1=o.P3[:, :], op=ALU.add), waits=(d_ev,), inc="sv")

    for e_ in range(NE):
        row_ev = b.step(G, lambda e, e_=e_: e.dma_start(
            out=o.t_row[:, :], in_=o.cwT_sb[e_:e_ + 1, :]),
            waits=(cw_ev, state["p3_free"], b.ev("st")), inc="sw", amt=16)

        def mm_cwb(e):
            ins = None
            for (a0, a1) in _tslices(HALF):
                ins = e.matmul(o.P3[:, a0:a1], o.onr_sb[:, :],
                               o.t_row[0:1, a0:a1], start=True, stop=True)
            return ins
        bc = b.step(T, mm_cwb, waits=(row_ev, state["p3_free"]), inc="st")
        cwb_ev = b.step(V, lambda e: e.tensor_copy(o.cwB_sb[:, :], o.P3[:, :]),
                        waits=(bc,), inc="sv")
        state["p3_free"] = cwb_ev
        expert_pass(None, NI, lambda i, e_=e_: up_ap(e_, i),
                    lambda h, e_=e_: dn_ap(e_, h), cwb_ev)
    for sp in range(2):
        expert_pass(0.5, NI, lambda i, sp=sp: sup_ap(sp * NI + i),
                    lambda h, sp=sp: sdn_ap(h, sp), None)
    return b.ev("sv")


def _ffn_block(b, o, x_ev, up_st, dn_st, sup_st, sdn_st):
    ln_ev = emit_ln(b, o, x_ev)
    cw_ev = emit_gating(b, o, ln_ev)
    return emit_ffn(
        b, o, ln_ev, cw_ev,
        lambda e_, i: up_st[e_, i],
        lambda e_, h: dn_st[e_, h],
        lambda i: sup_st[i],
        lambda h, sp: sdn_st[h, :, sp * NI:(sp + 1) * NI, :],
    )


def build_launch1():
    nc = bass.Bass("TRN2", target_bir_lowering=False)
    D = nc.dram_tensor
    x0 = D("x0", [H, TPC], F32, kind="ExternalInput").ap()
    up_st = D("up_st", [NE, NI, 128, 2, HC, 128], F32, kind="ExternalInput").ap()
    dn_st = D("dn_st", [NE, HC, 128, NI, 128], F32, kind="ExternalInput").ap()
    sup_st = D("sup_st", [NSI, 128, 2, HC, 128], F32, kind="ExternalInput").ap()
    sdn_st = D("sdn_st", [HC, 128, NSI, 128], F32, kind="ExternalInput").ap()
    gate_st = D("gate_st", [128, HC * NE], F32, kind="ExternalInput").ap()
    wq_st = D("wq_st", [HC, 128, HC, 128], F32, kind="ExternalInput").ap()
    wk_st = D("wk_st", [HC, 128, HC, 128], F32, kind="ExternalInput").ap()
    wv_st = D("wv_st", [4, 128, HC, 512], F32, kind="ExternalInput").ap()
    bq_st = D("bq_st", [HC, 128, 1], F32, kind="ExternalInput").ap()
    bk_st = D("bk_st", [HC, 128, 1], F32, kind="ExternalInput").ap()
    bv_st = D("bv_st", [1, H], F32, kind="ExternalInput").ap()
    id_st = D("id_st", [128, 128], F32, kind="ExternalInput").ap()
    onr_st = D("onr_st", [1, 128], F32, kind="ExternalInput").ap()
    onc_st = D("onc_st", [128, 1], F32, kind="ExternalInput").ap()

    x1_o = D("x1", [H, TPC], F32, kind="ExternalOutput").ap()
    q_o = D("q", [HC, 128, TPC], BF16, kind="ExternalOutput").ap()
    k_o = D("k", [HC, 128, TPC], BF16, kind="ExternalOutput").ap()
    v_o = D("v", [TPC, H], BF16, kind="ExternalOutput").ap()

    b = Sched(nc)
    o = alloc_common(nc)
    A = nc.alloc_sbuf_tensor
    c0 = o.bf_used
    wqk_sb = [o._bv(c0, c0 + HC * 128), o._bv(c0 + HC * 128, c0 + 2 * HC * 128)]
    c0 += 2 * HC * 128
    wv_sb = o._bv(c0, c0 + HC * 256); c0 += HC * 256
    qk_stage = [o._bv(c0, c0 + HALF), o._bv(c0 + HALF, c0 + 2 * HALF)]
    c0 += 2 * HALF
    v_stage = [o._bv(c0, c0 + 512), o._bv(c0 + 512, c0 + 1024)]
    c0 += 1024
    assert c0 <= o.AB.shape[1], (c0, o.AB.shape)
    bvB_sb = A("bvB_sb", [128, H], F32)
    bq_t = [A(f"bq{j}", [128, 1], F32) for j in range(HC)]
    bk_t = [A(f"bk{j}", [128, 1], F32) for j in range(HC)]

    G, T, V, S, SY = "gpsimd", "tensor", "vector", "scalar", "sync"

    consts = [(o.gate_sb[:, :], gate_st), (o.id_sb[:, :], id_st),
              (o.onr_sb[:, :], onr_st), (o.onc_sb[:, :], onc_st)]
    consts += [(bq_t[hc][:, :], bq_st[hc]) for hc in range(HC)]
    consts += [(bk_t[hc][:, :], bk_st[hc]) for hc in range(HC)]
    for (dst, src) in consts:
        consts_ev = b.step(G, lambda e, dst=dst, src=src: e.dma_start(out=dst, in_=src),
                           inc="sw", amt=16)

    prev = consts_ev
    for ch in range(H // HALF):
        ld = b.step(G, lambda e, ch=ch: e.dma_start(
            out=o.v_row[:, :], in_=bv_st[0:1, ch * HALF:(ch + 1) * HALF]),
            waits=(prev,), inc="sw", amt=16)
        prev2 = ld
        for (a0, a1) in _tslices(HALF):
            mm_ev = b.step(T, lambda e, a0=a0, a1=a1: e.matmul(
                o.P3[:, 0:a1 - a0], o.onr_sb[:, :], o.v_row[0:1, a0:a1],
                start=True, stop=True), waits=(prev2,), inc="st")
            prev2 = b.step(V, lambda e, ch=ch, a0=a0, a1=a1: e.tensor_copy(
                bvB_sb[:, ch * HALF + a0:ch * HALF + a1], o.P3[:, 0:a1 - a0]),
                waits=(mm_ev,), inc="sv")
        prev = prev2
    bvb_ev = prev

    x3d = x0.rearrange("(hc p) T -> p hc T", p=128)
    x1_3d = x1_o.rearrange("(hc p) T -> p hc T", p=128)

    for half in range(2):
        toff = half * HALF
        x_ev = b.step(SY, lambda e, toff=toff: e.dma_start(
            out=o.x_sb[:, :].rearrange("p (hc t) -> p hc t", hc=HC),
            in_=x3d[:, :, toff:toff + HALF]),
            waits=(bvb_ev, b.ev("sv"), b.ev("ss")), inc="sx", amt=16)
        ffn_ev = _ffn_block(b, o, x_ev, up_st, dn_st, sup_st, sdn_st)
        b.step(SY, lambda e, toff=toff: e.dma_start(
            out=x1_3d[:, :, toff:toff + HALF],
            in_=o.x_sb[:, :].rearrange("p (hc t) -> p hc t", hc=HC)),
            waits=(ffn_ev,), inc="sx", amt=16)
        ln2_ev = emit_ln(b, o, ffn_ev)

        for (w_ap, b_t, out_ap) in ((wq_st, bq_t, q_o), (wk_st, bk_t, k_o)):
            wdma_free = [b.ev("st"), b.ev("st")]
            stage_free = [b.ev("sx"), b.ev("sx")]
            for hc in range(HC):
                j = hc % 2
                wd = b.step(G, lambda e, hc=hc, j=j, w_ap=w_ap: e.dma_start(
                    out=wqk_sb[j][:, :], in_=w_ap[hc].rearrange("p k r -> p (k r)")),
                    waits=(wdma_free[j],), inc="sw", amt=16)

                def mm_qk(e, j=j):
                    ins = None
                    for kf in range(HC):
                        for (a0, a1) in _tslices(HALF):
                            ins = e.matmul(o.P1[:, a0:a1],
                                           wqk_sb[j][:, kf * 128:(kf + 1) * 128],
                                           o.xn_sb[:, kf * HALF + a0: kf * HALF + a1],
                                           start=(kf == 0), stop=(kf == HC - 1))
                    return ins
                mm_ev = b.step(T, mm_qk, waits=(wd, ln2_ev, b.ev("ss")), inc="st")
                wdma_free[j] = mm_ev
                ev_ev = b.step(S, lambda e, hc=hc, j=j, b_t=b_t: e.activation(
                    out=qk_stage[j][:, :], in_=o.P1[:, :], func=AF.Identity,
                    bias=b_t[hc][:, :], scale=1.0), waits=(mm_ev, stage_free[j]), inc="ss")
                stage_free[j] = b.step(SY, lambda e, hc=hc, j=j, out_ap=out_ap, toff=toff:
                                       e.dma_start(out=out_ap[hc, :, toff:toff + HALF],
                                                   in_=qk_stage[j][:, :]),
                                       waits=(ev_ev,), inc="sx", amt=16)

        wv_free = b.ev("st")
        vstage_free = [b.ev("sx"), b.ev("sx")]
        for ds2 in range(8):
            wd = b.step(G, lambda e, ds2=ds2: e.dma_start(
                out=wv_sb[:, :].rearrange("p (k r) -> p k r", k=HC),
                in_=wv_st[ds2 // 2][:, :, (ds2 % 2) * 256:(ds2 % 2 + 1) * 256]),
                waits=(wv_free,), inc="sw", amt=16)
            for tc in range(HALF // 128):
                j = tc % 2

                def mm_v(e, tc=tc):
                    ins = None
                    for kf in range(HC):
                        ins = e.matmul(o.P2[:, 0:256],
                                       o.xn_sb[:, kf * HALF + tc * 128: kf * HALF + (tc + 1) * 128],
                                       wv_sb[:, kf * 256:(kf + 1) * 256],
                                       start=(kf == 0), stop=(kf == HC - 1))
                    return ins
                mm_ev = b.step(T, mm_v, waits=(wd, ln2_ev, b.ev("sv")), inc="st")
                wv_free = mm_ev
                add_ev = b.step(V, lambda e, ds2=ds2, j=j: e.tensor_tensor(
                    out=v_stage[j][:, 0:256], in0=o.P2[:, 0:256],
                    in1=bvB_sb[:, ds2 * 256:(ds2 + 1) * 256], op=ALU.add),
                    waits=(mm_ev, vstage_free[j]), inc="sv")
                vstage_free[j] = b.step(SY, lambda e, ds2=ds2, tc=tc, j=j, toff=toff:
                                        e.dma_start(
                    out=v_o[toff + tc * 128: toff + (tc + 1) * 128,
                            ds2 * 256:(ds2 + 1) * 256], in_=v_stage[j][:, 0:256]),
                    waits=(add_ev,), inc="sx", amt=16)

    b.step(SY, lambda e: None,
           waits=(b.ev("sx"), b.ev("sv"), b.ev("ss"), b.ev("st"), b.ev("sw")))
    with nc.Block() as block:
        b.emit(block)
    return nc


def build_launch2():
    S_PAIR = 2 * TPC
    NKC = S_PAIR // 128
    QS = min(512, TPC)
    nc = bass.Bass("TRN2", target_bir_lowering=False)
    D = nc.dram_tensor
    x1 = D("x1", [H, TPC], F32, kind="ExternalInput").ap()
    q_i = D("q", [HC, 128, TPC], BF16, kind="ExternalInput").ap()
    k_i = D("kp", [HEADS, 128, S_PAIR], BF16, kind="ExternalInput").ap()
    v_i = D("vp", [HEADS, S_PAIR, 128], BF16, kind="ExternalInput").ap()
    wo_st = D("wo_st", [HC, 128, HC, 128], F32, kind="ExternalInput").ap()
    bo_st = D("bo_st", [HC, 128, 1], F32, kind="ExternalInput").ap()
    up_st = D("up_st", [NE, NI, 128, 2, HC, 128], F32, kind="ExternalInput").ap()
    dn_st = D("dn_st", [NE, HC, 128, NI, 128], F32, kind="ExternalInput").ap()
    sup_st = D("sup_st", [NSI, 128, 2, HC, 128], F32, kind="ExternalInput").ap()
    sdn_st = D("sdn_st", [HC, 128, NSI, 128], F32, kind="ExternalInput").ap()
    gate_st = D("gate_st", [128, HC * NE], F32, kind="ExternalInput").ap()
    id_st = D("id_st", [128, 128], F32, kind="ExternalInput").ap()
    onr_st = D("onr_st", [1, 128], F32, kind="ExternalInput").ap()
    onc_st = D("onc_st", [128, 1], F32, kind="ExternalInput").ap()

    x3_o = D("x3", [H, TPC], F32, kind="ExternalOutput").ap()
    x2_d = D("x2_scratch", [H, TPC], F32).ap()

    b = Sched(nc)
    nb_attn = HEADS * QS + 2 * S_PAIR + 2 * NKC * 128 + NKC * QS + HC * QS + 2 * HC * 128
    o = alloc_common(nc, nb_min=nb_attn)
    A = nc.alloc_sbuf_tensor
    c0 = 0
    def take(n):
        nonlocal c0
        v = o._bv(c0, c0 + n); c0 += n
        return v
    q_sb = take(HEADS * QS)
    kh_sb = [take(S_PAIR), take(S_PAIR)]
    vh_sb = [take(NKC * 128), take(NKC * 128)]
    pT_sb = take(NKC * QS)
    ao_sb = take(HC * QS)
    wo_sb = [take(HC * 128), take(HC * 128)]
    assert c0 <= o.AB.shape[1], (c0, o.AB.shape)
    linv_sb = o.AF[:, 0:QS]
    x1t_sb = [o.AF[:, QS:2 * QS], o.AF[:, 2 * QS:3 * QS]]
    x2t_sb = [o.AF[:, 3 * QS:4 * QS], o.AF[:, 4 * QS:5 * QS]]
    bo_t = [A(f"bo{j}", [128, 1], F32) for j in range(HC)]

    G, T, V, S, SY = "gpsimd", "tensor", "vector", "scalar", "sync"

    consts = [(o.gate_sb[:, :], gate_st), (o.id_sb[:, :], id_st),
              (o.onr_sb[:, :], onr_st), (o.onc_sb[:, :], onc_st)]
    consts += [(bo_t[hc][:, :], bo_st[hc]) for hc in range(HC)]
    for (dst, src) in consts:
        consts_ev = b.step(G, lambda e, dst=dst, src=src: e.dma_start(out=dst, in_=src),
                           inc="sw", amt=16)

    assert HALF >= 1024 and QS == 512
    pS_t = [o.P1[:, 0:512], o.P1[:, 512:1024]]
    pL_t = o.P2[0:1, 0:512]
    pV_t = o.P2[:, 512:1024]
    pB_t = o.P3[:, 0:512]
    pPr_t = o.P3[:, 512:1024]

    kh_free = [b.ev("st"), b.ev("st")]
    vh_free = [b.ev("st"), b.ev("st")]
    q_free = b.ev("st")
    ao_free = b.ev("st")
    x1t_free = [b.ev("sv"), b.ev("sv")]
    x2t_free = [b.ev("sx"), b.ev("sx")]
    wo_free = [b.ev("st"), b.ev("st")]

    for qs in range(TPC // QS):
        q_ev = b.step(SY, lambda e, qs=qs: e.dma_start(
            out=q_sb[:, :].rearrange("p (h t) -> p h t", h=HEADS),
            in_=q_i[:, :, qs * QS:(qs + 1) * QS].transpose([1, 0, 2])),
            waits=(q_free,), inc="sx", amt=16)
        for h in range(HEADS):
            jh = h % 2
            kh_ev = b.step(G, lambda e, h=h, jh=jh: e.dma_start(
                out=kh_sb[jh][:, :], in_=k_i[h]), waits=(kh_free[jh],), inc="sw", amt=16)
            vh_ev = b.step(G, lambda e, h=h, jh=jh: e.dma_start(
                out=vh_sb[jh][:, :].rearrange("p (kc d) -> p kc d", kc=NKC),
                in_=v_i[h].rearrange("(kc p) d -> p kc d", p=128)),
                waits=(vh_free[jh],), inc="sw", amt=16)
            exp_last = None
            pS_free = [b.ev("ss"), b.ev("ss")]
            for kc in range(NKC):
                jk = kc % 2
                mm_s = b.step(T, lambda e, kc=kc, jh=jh, jk=jk, h=h: e.matmul(
                    pS_t[jk], kh_sb[jh][:, kc * 128:(kc + 1) * 128],
                    q_sb[:, h * QS:(h + 1) * QS], start=True, stop=True),
                    waits=(kh_ev, q_ev, pS_free[jk]), inc="st")
                pS_free[jk] = b.step(S, lambda e, kc=kc, jk=jk: e.activation(
                    out=pT_sb[:, kc * QS:(kc + 1) * QS], in_=pS_t[jk],
                    func=AF.Exp, scale=SCALE),
                    waits=(mm_s,), inc="ss")
                exp_last = pS_free[jk]

            def mm_l(e):
                ins = None
                for kc in range(NKC):
                    ins = e.matmul(pL_t, o.onc_sb[:, :], pT_sb[:, kc * QS:(kc + 1) * QS],
                                   start=(kc == 0), stop=(kc == NKC - 1))
                return ins
            l_ev = b.step(T, mm_l, waits=(exp_last, b.ev("sv")), inc="st")

            def mm_pv(e, jh=jh):
                ins = None
                for kc in range(NKC):
                    ins = e.matmul(pV_t, vh_sb[jh][:, kc * 128:(kc + 1) * 128],
                                   pT_sb[:, kc * QS:(kc + 1) * QS],
                                   start=(kc == 0), stop=(kc == NKC - 1))
                return ins
            pv_ev = b.step(T, mm_pv, waits=(vh_ev, exp_last, b.ev("sv")), inc="st")
            kh_free[jh] = l_ev
            vh_free[jh] = pv_ev
            r1 = b.step(V, lambda e: e.reciprocal(out=o.m_row[0:1, 0:QS], in_=pL_t),
                        waits=(l_ev,), inc="sv")
            bc = b.step(T, lambda e: e.matmul(pB_t, o.onr_sb[:, :],
                                              o.m_row[0:1, 0:QS], start=True, stop=True),
                        waits=(r1,), inc="st")
            cp = b.step(V, lambda e: e.tensor_copy(linv_sb[:, :], pB_t),
                        waits=(bc,), inc="sv")
            b.step(V, lambda e, h=h: e.tensor_tensor(
                out=ao_sb[:, h * QS:(h + 1) * QS], in0=pV_t, in1=linv_sb[:, :],
                op=ALU.mult), waits=(cp, pv_ev, ao_free), inc="sv")
        ao_all = b.ev("sv")
        q_free = ao_all
        for hc in range(HC):
            j = hc % 2
            wd = b.step(G, lambda e, hc=hc, j=j: e.dma_start(
                out=wo_sb[j][:, :], in_=wo_st[hc].rearrange("p k r -> p (k r)")),
                waits=(wo_free[j],), inc="sw", amt=16)
            x1l = b.step(SY, lambda e, hc=hc, j=j, qs=qs: e.dma_start(
                out=x1t_sb[j][:, :],
                in_=x1[hc * 128:(hc + 1) * 128, qs * QS:(qs + 1) * QS]),
                waits=(x1t_free[j],), inc="sx", amt=16)

            def mm_pr(e, j=j):
                ins = None
                for cc in range(HC):
                    ins = e.matmul(pPr_t, wo_sb[j][:, cc * 128:(cc + 1) * 128],
                                   ao_sb[:, cc * QS:(cc + 1) * QS],
                                   start=(cc == 0), stop=(cc == HC - 1))
                return ins
            pr_ev = b.step(T, mm_pr, waits=(wd, ao_all, b.ev("sv")), inc="st")
            wo_free[j] = pr_ev
            ad1 = b.step(V, lambda e, j=j: e.tensor_tensor(
                out=x2t_sb[j][:, :], in0=x1t_sb[j][:, :], in1=pPr_t, op=ALU.add),
                waits=(pr_ev, x1l, x2t_free[j]), inc="sv")
            x1t_free[j] = ad1
            ad2 = b.step(V, lambda e, j=j, hc=hc: e.tensor_scalar(
                out=x2t_sb[j][:, :], in0=x2t_sb[j][:, :], scalar1=bo_t[hc][:, :],
                scalar2=None, op0=ALU.add), waits=(ad1,), inc="sv")
            x2t_free[j] = b.step(SY, lambda e, hc=hc, j=j, qs=qs: e.dma_start(
                out=x2_d[hc * 128:(hc + 1) * 128, qs * QS:(qs + 1) * QS],
                in_=x2t_sb[j][:, :]), waits=(ad2,), inc="sx", amt=16)
        ao_free = b.ev("st")

    attn_done = (b.ev("sx"), b.ev("sv"), b.ev("st"), b.ev("ss"), b.ev("sw"))

    x2_3d = x2_d.rearrange("(hc p) T -> p hc T", p=128)
    x3_3d = x3_o.rearrange("(hc p) T -> p hc T", p=128)
    for half in range(2):
        toff = half * HALF
        x_ev = b.step(SY, lambda e, toff=toff: e.dma_start(
            out=o.x_sb[:, :].rearrange("p (hc t) -> p hc t", hc=HC),
            in_=x2_3d[:, :, toff:toff + HALF]),
            waits=attn_done + (b.ev("sv"), b.ev("ss")), inc="sx", amt=16)
        ffn_ev = _ffn_block(b, o, x_ev, up_st, dn_st, sup_st, sdn_st)
        b.step(SY, lambda e, toff=toff: e.dma_start(
            out=x3_3d[:, :, toff:toff + HALF],
            in_=o.x_sb[:, :].rearrange("p (hc t) -> p hc t", hc=HC)),
            waits=(ffn_ev,), inc="sx", amt=16)

    b.step(SY, lambda e: None,
           waits=(b.ev("sx"), b.ev("sv"), b.ev("ss"), b.ev("st"), b.ev("sw")))
    with nc.Block() as block:
        b.emit(block)
    return nc


def _run(inputs, x_full):
    st = _stage_all(inputs)
    common = dict(up_st=st["up"], dn_st=st["dn"], sup_st=st["sup"], sdn_st=st["sdn"],
                  gate_st=st["gate"], id_st=st["ident"], onr_st=st["ones_row"],
                  onc_st=st["ones_col"])

    nc1 = build_launch1()
    in1 = []
    for c in range(NCORES):
        x0 = np.ascontiguousarray(x_full[c * TPC:(c + 1) * TPC].T)
        in1.append(dict(x0=x0, wq_st=st["wq"], wk_st=st["wk"], wv_st=st["wv"],
                        bq_st=st["bq"], bk_st=st["bk"], bv_st=st["bv"], **common))
    r1 = run_bass_kernel_spmd(nc1, in1, core_ids=list(range(NCORES)))
    res1 = r1.results

    nc2 = build_launch2()
    in2 = []
    for c in range(NCORES):
        pair = (c // 2) * 2
        k_pair = np.concatenate([res1[pair]["k"], res1[pair + 1]["k"]], axis=2)
        v_pair = np.ascontiguousarray(
            np.concatenate([res1[pair]["v"].reshape(TPC, HEADS, DH),
                            res1[pair + 1]["v"].reshape(TPC, HEADS, DH)],
                           axis=0).transpose(1, 0, 2))
        in2.append(dict(x1=res1[c]["x1"], q=res1[c]["q"], kp=k_pair, vp=v_pair,
                        wo_st=st["wo"], bo_st=st["bo"], **common))
    r2 = run_bass_kernel_spmd(nc2, in2, core_ids=list(range(NCORES)))
    return np.concatenate([r2.results[c]["x3"].T for c in range(NCORES)], axis=0)


def kernel(**inputs):
    X = np.asarray(inputs["L0_attn_x"], np.float32)
    B, S = X.shape[0], X.shape[1]
    out = _run(inputs, X.reshape(B * S, H))
    return np.ascontiguousarray(out.reshape(B, S, H))
